# version 14
# baseline (speedup 1.0000x reference)
import numpy as np
import ml_dtypes
from contextlib import ExitStack

import concourse.bass as bass
import concourse.mybir as mybir
from concourse.bass_utils import run_bass_kernel_spmd

F32 = mybir.dt.float32
BF16 = mybir.dt.bfloat16
U8 = mybir.dt.uint8
AF = mybir.ActivationFunctionType
ALU = mybir.AluOpType
AX = mybir.AxisListType

S = 2048
DM = 2048
NT = 16
DT = 16
HEADS = 16
ET = 18
IT = 11
UNITS = 20
NSC = 2
SCT = 8
QS = float(1.0 / np.sqrt(128.0))
EPS = 1e-5
NEGINF = -1e30
ARENA_B = 205312

ENGINES = ("sync", "gpsimd", "tensor", "vector", "scalar")


class Sem:
    def __init__(self, h):
        self.h = h
        self.n = 0


class Prog:
    def __init__(self):
        self.q = {e: [] for e in ENGINES}
        self.tag = "init"
        self.tagmap = {}

    def add(self, eng, meth, *args, waits=(), inc=None, inck=1, **kw):
        val = None
        if inc is not None:
            inc.n += inck
            val = inc.n
        wlist = [(s.h, v) for (s, v) in waits if v > 0]
        ih = inc.h if inc is not None else None
        tag = self.tag
        tm = self.tagmap

        def fn(e, meth=meth, args=args, kw=kw, wlist=wlist, ih=ih, inck=inck):
            for h, v in wlist:
                e.wait_ge(h, v)
            if meth is None:
                return
            ins = getattr(e, meth)(*args, **kw)
            if ih is not None:
                ins.then_inc(ih, inck)
            try:
                tm[ins.ins.name] = tag
            except AttributeError:
                try:
                    tm[ins.name] = tag
                except Exception:
                    pass
        self.q[eng].append(fn)
        return val


class Arena:
    def __init__(self, nc):
        self.h8 = nc.alloc_sbuf_tensor("arena", [128, ARENA_B], U8)
        self.hb = self.h8.bitcast(BF16).ap()
        self.hf = self.h8.bitcast(F32).ap()
        self.off = 0
        self.marks = []
        self.peak = 0

    def push(self):
        self.marks.append(self.off)

    def pop(self):
        self.off = self.marks.pop()

    def alloc(self, nbytes, align=512):
        o = (self.off + align - 1) // align * align
        self.off = o + nbytes
        self.peak = max(self.peak, self.off)
        assert self.off <= ARENA_B, f"arena overflow: {self.off} > {ARENA_B}"
        return o

    def f32(self, cols, align=512):
        o = self.alloc(cols * 4, align)
        return self.hf[:, o // 4:o // 4 + cols]

    def bf16(self, cols, align=512):
        o = self.alloc(cols * 2, align)
        return self.hb[:, o // 2:o // 2 + cols]

    def f32_at(self, byte_off, cols):
        return self.hf[:, byte_off // 4:byte_off // 4 + cols]


_LAST = {}


def build_program(dbg=False, phases=3, units_limit=UNITS):
    nc = bass.Bass("TRN2", target_bir_lowering=False, debug=False)

    x_d = nc.dram_tensor("x", [S, DM], F32, kind="ExternalInput").ap()
    wqkv_d = nc.dram_tensor("wqkv", [ET, 128, 2048], F32, kind="ExternalInput").ap()
    wqkvb_d = nc.dram_tensor("wqkvb", [128, ET], F32, kind="ExternalInput").ap()
    wout_d = nc.dram_tensor("wout", [DT, 128, 2048], F32, kind="ExternalInput").ap()
    woutb_d = nc.dram_tensor("woutb", [1, 2048], F32, kind="ExternalInput").ap()
    wrt_d = nc.dram_tensor("wrt", [128, DT * 17], F32, kind="ExternalInput").ap()
    wgu_d = nc.dram_tensor("wgu", [UNITS, IT, 128, 4096], F32, kind="ExternalInput").ap()
    wdn_d = nc.dram_tensor("wdn", [UNITS, IT, 128, 2048], F32, kind="ExternalInput").ap()
    ident_d = nc.dram_tensor("ident", [128, 128], BF16, kind="ExternalInput").ap()
    ones_d = nc.dram_tensor("ones", [1, 128], BF16, kind="ExternalInput").ap()
    masks_d = nc.dram_tensor("masks", [128, 4 * 512], BF16, kind="ExternalInput").ap()
    y_d = nc.dram_tensor("y", [S, DM], F32, kind="ExternalOutput").ap()
    xres_d = nc.dram_tensor("xres", [S, DM], F32,
                            kind="ExternalOutput" if dbg else "Internal").ap()
    if dbg:
        dqkv_d = nc.dram_tensor("dbg_qkvf", [ET * 128, 2048], BF16, kind="ExternalOutput").ap()
        dxn1_d = nc.dram_tensor("dbg_xn1f", [DT * 128, 2048], BF16, kind="ExternalOutput").ap()

    prog = Prog()
    _LAST['tagmap'] = prog.tagmap

    with ExitStack() as top:
        ar = Arena(nc)
        ps8h = top.enter_context(nc.psum_tensor("ps8", [128, 4096], F32))
        psF = ps8h.ap()
        psH = ps8h.bitcast(BF16).ap()

        def PSB(b):
            return psH[:, b * 1024: b * 1024 + 128]

        def PSF(b, n=1):
            return psF[:, b * 512:(b + n) * 512]

        def mksem(name):
            return Sem(top.enter_context(nc.semaphore(name)))
        SXC = mksem("SXC")
        SXT = [mksem(f"SXT{i}") for i in range(2)]
        SXE = [mksem(f"SXE{i}") for i in range(2)]
        SXACC = mksem("SXACC")
        SXD = mksem("SXD")
        SWC = mksem("SWC")
        SWQ = [mksem(f"SWQ{i}") for i in range(4)]
        SOW = [mksem(f"SOW{i}") for i in range(4)]
        SGU = [mksem(f"SGU{i}") for i in range(2)]
        SWDN = mksem("SWDN")
        SSX = [mksem(f"SSX{i}") for i in range(2)]
        SSY = mksem("SSY")
        PE = mksem("PE")
        VE = mksem("VE")
        SE = mksem("SE")
        BAR = mksem("BAR")

        ident_sb = ar.bf16(128)
        masks_off = ar.alloc(4096)
        masks_sb = ar.hb[:, masks_off // 2: masks_off // 2 + 2048]
        qkvb_sb = ar.f32(ET, align=64)
        wrt_sb = ar.bf16(DT * 17, align=64)
        rowa = ar.alloc(4096 + 256)
        woutb_row = ar.hb[0:1, rowa // 2: rowa // 2 + 2048]
        ones_row = ar.hb[0:1, rowa // 2 + 2048: rowa // 2 + 2048 + 128]
        stat = ar.f32(64, align=64)
        cmb = ar.f32(SCT * 16, align=64)
        sgl = ar.f32(SCT, align=64)
        rtb = ar.f32(SCT * 16, align=64)
        ebuf = ar.f32(16, align=64)
        e1buf = ar.f32(16, align=64)
        tbuf = ar.f32(16, align=64)
        xn_tok = [ar.bf16(2048) for _ in range(2)]

        def stat_c(par, k, n=1):
            c = par * 32 + k
            return stat[:, c:c + n]

        prog.add("sync", "dma_start", out=ident_sb, in_=ident_d, inc=SXC, inck=16)
        prog.add("sync", "dma_start", out=masks_sb, in_=masks_d, inc=SXC, inck=16)
        prog.add("sync", "dma_start", out=qkvb_sb, in_=wqkvb_d, inc=SXC, inck=16)
        const_sx = (SXC, SXC.n)
        prog.add("gpsimd", "dma_start", out=wrt_sb, in_=wrt_d, inc=SWC, inck=16)
        prog.add("gpsimd", "dma_start", out=woutb_row, in_=woutb_d, inc=SWC, inck=16)
        prog.add("gpsimd", "dma_start", out=ones_row, in_=ones_d, inc=SWC, inck=16)
        const_sw = (SWC, SWC.n)

        def barrier():
            tgt = BAR.n + 5
            for e in ENGINES:
                prog.add(e, "nop", inc=BAR)
            BAR.n = tgt
            for e in ENGINES:
                prog.add(e, None, waits=[(BAR, tgt)])

        tr_ev = {0: 0, 1: 0}

        def vsync(meth, *args, waits=(), **kw):
            return prog.add("vector", meth, *args,
                            waits=[(VE, VE.n)] + list(waits), inc=VE, **kw)

        def emit_ln(par, src_f32, dst_bf, trash_f32, wait_src, dst_war=()):
            s1 = stat_c(par, 0); s2 = stat_c(par, 1); mcol = stat_c(par, 2)
            msq = stat_c(par, 3); var = stat_c(par, 4); rvar = stat_c(par, 5)
            rstd = stat_c(par, 6); negmr = stat_c(par, 7)
            n1 = prog.add("vector", "reduce_sum", s1, src_f32, axis=AX.X,
                          waits=wait_src, inc=VE)
            se_s2 = prog.add("scalar", "activation", trash_f32, src_f32, AF.Square,
                             accum_out=s2, waits=wait_src, inc=SE)
            n2 = prog.add("vector", "tensor_scalar_mul", mcol, s1, 1.0 / DM,
                          waits=[(SE, se_s2), (VE, n1)], inc=VE)
            n3 = prog.add("vector", "tensor_mul", msq, mcol, mcol,
                          waits=[(VE, n2)], inc=VE)
            n4 = prog.add("vector", "scalar_tensor_tensor", var, s2, 1.0 / DM, msq,
                          ALU.mult, ALU.subtract, waits=[(VE, n3)], inc=VE)
            n5 = prog.add("vector", "tensor_scalar_add", var, var, EPS,
                          waits=[(VE, n4)], inc=VE)
            n6 = prog.add("vector", "reciprocal", rvar, var, waits=[(VE, n5)], inc=VE)
            se_rs = prog.add("scalar", "activation", rstd, rvar, AF.Sqrt,
                             waits=[(VE, n6)], inc=SE)
            n7 = prog.add("vector", "scalar_tensor_tensor", negmr, mcol, -1.0,
                          rstd, ALU.mult, ALU.mult,
                          waits=[(SE, se_rs), (VE, n6)], inc=VE)
            return prog.add("scalar", "activation", dst_bf, src_f32, AF.Identity,
                            bias=negmr, scale=rstd,
                            waits=[(VE, n7)] + list(dst_war), inc=SE)

        def emit_transpose16(src_bf, dst_tiles, se_ready):
            last_pe = 0
            for t in range(DT):
                b = t % 2
                w = [(VE, tr_ev[b])]
                if t == 0:
                    w.append((SE, se_ready))
                last_pe = prog.add("tensor", "transpose", PSB(b),
                                   src_bf[:, t * 128:(t + 1) * 128], ident_sb,
                                   waits=w, inc=PE)
                tr_ev[b] = prog.add("vector", "tensor_copy", dst_tiles[t], PSB(b),
                                    waits=[(PE, last_pe)], inc=VE)
            return max(tr_ev[0], tr_ev[1]), last_pe

        ar.push()
        qkvf = [ar.bf16(2048) for _ in range(ET)]
        v_tok = ar.bf16(2048)
        ar.push()
        xn1f = [ar.bf16(2048) for _ in range(DT)]
        wq_ring = [ar.bf16(2048) for _ in range(4)]
        sq_trash = ar.f32(2048)
        xt_ring = [ar.f32(2048) for _ in range(2)]

        se_lnA = [0] * NT
        pe_trA = [0] * NT
        xa_done = [0] * NT
        def emit_trA(i):
            dsts = [xn1f[t][:, i * 128:(i + 1) * 128] for t in range(DT)]
            xa_done[i], pe_trA[i] = emit_transpose16(xn_tok[i % 2], dsts, se_lnA[i])
            if i > 0:
                xa_done[i] = max(xa_done[i], xa_done[i - 1])

        for i in range(NT):
            prog.tag = f'A.ln i={i}'
            w = [const_sx] if i < 2 else [(SE, se_lnA[i - 2])]
            sx = prog.add("sync", "dma_start", out=xt_ring[i % 2],
                          in_=x_d[i * 128:(i + 1) * 128, :], waits=w,
                          inc=SXT[i % 2], inck=16)
            dst_war = [(PE, pe_trA[i - 2])] if i >= 2 else []
            se_lnA[i] = emit_ln(i % 2, xt_ring[i % 2], xn_tok[i % 2], sq_trash,
                                [(SXT[i % 2], sx)], dst_war)
            if i >= 1:
                emit_trA(i - 1)
        emit_trA(NT - 1)
        xn1f_ready = max(tr_ev.values())

        sw_chunk = [0] * ET
        pe_grp = [0] * ET
        se_evB = [0] * ET
        for j in range(ET):
            prog.tag = f'B.qkv j={j}'
            w = [(PE, pe_grp[j - 4])] if j >= 4 else []
            sw_chunk[j] = prog.add("gpsimd", "dma_start", out=wq_ring[j % 4],
                                   in_=wqkv_d[j], waits=w, inc=SWQ[j % 4], inck=16)
            half = (j % 2) * 4
            for c in range(4):
                w = [(SWQ[j % 4], sw_chunk[j]), (VE, xa_done[4 * c + 3])]
                if j >= 2:
                    w.append((SE, se_evB[j - 2]))
                for t in range(DT):
                    pe_grp[j] = prog.add(
                        "tensor", "matmul", PSF(half + c),
                        wq_ring[j % 4][:, t * 128:(t + 1) * 128],
                        xn1f[t][:, c * 512:(c + 1) * 512],
                        start=(t == 0), stop=(t == DT - 1),
                        waits=(w if t == 0 else []),
                        inc=(PE if (t == DT - 1 and c == 3) else None)) or pe_grp[j]
            se_evB[j] = prog.add("scalar", "activation", qkvf[j], PSF(half, 4),
                                 AF.Identity, bias=qkvb_sb[:, j:j + 1],
                                 waits=[(PE, pe_grp[j])], inc=SE)
        dsts = [v_tok[:, t * 128:(t + 1) * 128] for t in range(DT)]
        vtok_ready, _ = emit_transpose16(qkvf[17], dsts, se_evB[17])

        if dbg:
            for j in range(ET):
                prog.add("sync", "dma_start", out=dqkv_d[j * 128:(j + 1) * 128, :],
                         in_=qkvf[j], waits=[(SE, se_evB[j]), (VE, vtok_ready)],
                         inc=SXD, inck=16)
            for t in range(DT):
                prog.add("sync", "dma_start", out=dxn1_d[t * 128:(t + 1) * 128, :],
                         in_=xn1f[t], waits=[(VE, xn1f_ready)], inc=SXD, inck=16)

        barrier()
        ar.pop()

        if phases < 2:
            prog.add("sync", None, waits=[(SXD, SXD.n)])
            barrier()
            _finish(nc, prog)
            return nc
        ar.push()
        ctxf = [ar.bf16(512) for _ in range(DT)]
        pT_all = ar.bf16(NT * 512)
        pT = [pT_all[:, kt * 512:(kt + 1) * 512] for kt in range(NT)]
        pT3 = pT_all.rearrange("p (kt c) -> p kt c", kt=NT)
        outw_ring = [ar.bf16(8 * 512) for _ in range(4)]
        p_sb = [ar.bf16(2048) for _ in range(4)]
        row_sb = [ar.f32(2048) for _ in range(2)]
        diag = [ar.bf16(128, align=64) for _ in range(4)]
        cstat = ar.f32(64, align=64)
        xt_s = [ar.f32(512) for _ in range(2)]

        se_exp = {}
        ve_pt = {}
        pe_pt = {}
        pe_ctx = {}
        ve_ctx_ev = {}
        ve_ttr = {}
        se_ttr = {}
        pe_op_grp = {}
        ve_op_ev = {}
        ss_xres = {}

        for qc in range(4):
            n_kt = 4 * (qc + 1)
            nkc = qc + 1
            L = nkc * 512
            tiles = [(h, ql) for h in range(HEADS) for ql in range(4)]
            bank_ev = {4: 0, 5: 0}
            ve_red = {}

            def emit_sc(idx):
                h, ql = tiles[idx]
                prog.tag = f'C.head qc={qc} h={h} ql={ql}'
                qi = qc * 4 + ql
                w = [(VE, vtok_ready)]
                if idx > 0:
                    w.append((SE, se_ttr[tiles[idx - 1]]))
                pe_sc = 0
                for kc in range(nkc):
                    pe_sc = prog.add("tensor", "matmul", PSF(kc),
                                     qkvf[h][:, qi * 128:(qi + 1) * 128],
                                     qkvf[16][:, kc * 512:(kc + 1) * 512],
                                     start=True, stop=(kc != qc),
                                     waits=(w if kc == 0 else []),
                                     inc=(None if kc == qc else PE))
                pe_sc = prog.add("tensor", "matmul", PSF(qc), ident_sb,
                                 masks_sb[:, (qi % 4) * 512:(qi % 4 + 1) * 512],
                                 start=False, stop=True, inc=PE)
                return pe_sc

            def emit_evac(idx, pe_sc):
                h, ql = tiles[idx]
                prog.tag = f'C.head qc={qc} h={h} ql={ql}'
                par = ql % 2
                se_ttr[(h, ql)] = prog.add(
                    "scalar", "activation", row_sb[par][:, 0:L], psF[:, 0:L],
                    AF.Identity, scale=QS, waits=[(PE, pe_sc)], inc=SE)

            def emit_red(idx):
                h, ql = tiles[idx]
                prog.tag = f'C.head qc={qc} h={h} ql={ql}'
                par = ql % 2
                negm = cstat[:, ql * 16 + 9:ql * 16 + 10]
                ve_red[(h, ql)] = prog.add(
                    "vector", "reduce_max", negm, row_sb[par][:, 0:L],
                    axis=AX.X, negate=True,
                    waits=[(SE, se_ttr[(h, ql)])], inc=VE)

            def emit_exp(idx):
                h, ql = tiles[idx]
                prog.tag = f'C.head qc={qc} h={h} ql={ql}'
                par = ql % 2
                negm = cstat[:, ql * 16 + 9:ql * 16 + 10]
                w = [(VE, ve_red[(h, ql)])]
                if idx >= 4 and tiles[idx - 4] in pe_pt:
                    w.append((PE, pe_pt[tiles[idx - 4]]))
                se_exp[(h, ql)] = prog.add(
                    "scalar", "activation", p_sb[ql][:, 0:L],
                    row_sb[par][:, 0:L], AF.Exp, bias=negm,
                    accum_out=cstat[:, ql * 16 + 10:ql * 16 + 11], waits=w, inc=SE)

            def emit_back(idx):
                h, ql = tiles[idx]
                prog.tag = f'C.pt qc={qc} h={h} ql={ql}'
                sume = cstat[:, ql * 16 + 10:ql * 16 + 11]
                rs = cstat[:, ql * 16 + 12:ql * 16 + 13]
                ve_rs = prog.add("vector", "reciprocal", rs, sume,
                                 waits=[(SE, se_exp[(h, ql)])], inc=VE)
                wdg = [(VE, ve_rs)]
                if idx >= 4 and tiles[idx - 4] in pe_pt:
                    wdg.append((PE, pe_pt[tiles[idx - 4]]))
                ve_dg = prog.add("vector", "tensor_scalar", diag[ql], ident_sb,
                                 rs, None, ALU.mult, waits=wdg, inc=VE)
                for kt in range(n_kt):
                    grp = kt // 4
                    b = 4 + (grp % 2)
                    w2 = [(VE, ve_dg)] if kt == 0 else []
                    if kt % 4 == 0 and bank_ev[b] > 0:
                        w2.append((VE, bank_ev[b]))
                    pe_pt[(h, ql)] = prog.add(
                        "tensor", "matmul",
                        psF[:, b * 512 + (kt % 4) * 128: b * 512 + (kt % 4 + 1) * 128],
                        p_sb[ql][:, kt * 128:(kt + 1) * 128], diag[ql],
                        start=True, stop=True, waits=w2, inc=PE)
                    if kt % 4 == 3 or kt == n_kt - 1:
                        nk = kt % 4 + 1
                        bank_ev[b] = prog.add(
                            "vector", "tensor_copy",
                            pT3[:, grp * 4:grp * 4 + nk, ql * 128:(ql + 1) * 128],
                            psF[:, b * 512:b * 512 + nk * 128]
                            .rearrange("p (kt c) -> p kt c", kt=nk),
                            waits=[(PE, pe_pt[(h, ql)])], inc=VE)
                if ql == 3:
                    prog.tag = f'C.ctx qc={qc} h={h}'
                    cb = 6 + (h % 2)
                    w = [(VE, VE.n)]
                    if (qc, h - 2) in ve_ctx_ev:
                        w.append((VE, ve_ctx_ev[(qc, h - 2)]))
                    for kt in range(n_kt):
                        pe_ctx[(qc, h)] = prog.add(
                            "tensor", "matmul", PSF(cb),
                            v_tok[:, kt * 128:(kt + 1) * 128], pT[kt],
                            start=(kt == 0), stop=(kt == n_kt - 1),
                            waits=(w if kt == 0 else []),
                            inc=(PE if kt == n_kt - 1 else None)) or pe_ctx.get((qc, h), 0)
                    ve_ctx_ev[(qc, h)] = prog.add(
                        "vector", "tensor_copy", ctxf[h], PSF(cb),
                        waits=[(PE, pe_ctx[(qc, h)])], inc=VE)

            NTL = len(tiles)
            for idx in range(NTL):
                pe_sc = emit_sc(idx)
                emit_evac(idx, pe_sc)
                if idx >= 2:
                    emit_back(idx - 2)
                emit_red(idx)
                if idx >= 1:
                    emit_exp(idx - 1)
            emit_exp(NTL - 1)
            emit_back(NTL - 2)
            emit_back(NTL - 1)
            ctx_ready = max(ve_ctx_ev[(qc, hh)] for hh in range(HEADS))
            for ec in range(4):
                prog.tag = f'C.op qc={qc} ec={ec}'
                sw_half = [0, 0]
                for half in range(2):
                    kk = (qc * 4 + ec) * 2 + half
                    w = []
                    kprev = (qc, ec - 2) if ec >= 2 else ((qc - 1, ec + 2) if qc > 0 else None)
                    if kprev is not None and kprev in pe_op_grp:
                        w.append((PE, pe_op_grp[kprev]))
                    if kk == 0:
                        w.append((SXD, SXD.n))
                    sw_half[half] = prog.add(
                        "gpsimd", "dma_start",
                        out=outw_ring[kk % 4].rearrange("p (t c) -> p t c", t=8),
                        in_=wout_d[half * 8:(half + 1) * 8, :,
                                   ec * 512:(ec + 1) * 512].transpose([1, 0, 2]),
                        waits=w, inc=SOW[kk % 4], inck=16)
                for si in range(4):
                    bank = si % 4
                    wx = []
                    kp = (qc, ec, si - 2) if si >= 2 else (
                        (qc, ec - 1, si + 2) if ec > 0 else
                        ((qc - 1, 3, si + 2) if qc > 0 else None))
                    if kp is not None and kp in ss_xres:
                        wx.append((SSX[si % 2], ss_xres[kp]))
                        wx.append((VE, ve_op_ev[kp]))
                    sx_x = prog.add("sync", "dma_start", out=xt_s[si % 2],
                                    in_=x_d[qc * 512 + si * 128:
                                            qc * 512 + (si + 1) * 128,
                                            ec * 512:(ec + 1) * 512],
                                    waits=wx, inc=SXE[si % 2], inck=16)
                    w = []
                    if si == 0:
                        w = [(SOW[((qc * 4 + ec) * 2) % 4], sw_half[0]), (VE, ctx_ready)]
                        if ec == 0 and qc == 0:
                            w.append(const_sw)
                        if ec > 0:
                            w.append((VE, ve_op_ev[(qc, ec - 1, 3)]))
                        elif qc > 0:
                            w.append((VE, ve_op_ev[(qc - 1, 3, 3)]))
                    prog.add("tensor", "matmul", PSF(bank), ones_row,
                             woutb_row[:, ec * 512:(ec + 1) * 512],
                             start=True, stop=False, waits=w)
                    for t in range(DT):
                        tw = []
                        if t == 8:
                            tw = [(SOW[(((qc * 4 + ec) * 2) + 1) % 4], sw_half[1])]
                        pe_op_grp[(qc, ec)] = prog.add(
                            "tensor", "matmul", PSF(bank),
                            ctxf[t][:, si * 128:(si + 1) * 128],
                            outw_ring[(((qc * 4 + ec) * 2) + t // 8) % 4][
                                :, (t % 8) * 512:(t % 8 + 1) * 512],
                            start=False, stop=(t == DT - 1), waits=tw,
                            inc=(PE if t == DT - 1 else None)) or pe_op_grp.get((qc, ec), 0)
                    ve_op_ev[(qc, ec, si)] = prog.add(
                        "vector", "tensor_tensor", xt_s[si % 2], PSF(bank),
                        xt_s[si % 2], ALU.add,
                        waits=[(PE, pe_op_grp[(qc, ec)]), (SXE[si % 2], sx_x)], inc=VE)
                    ss_xres[(qc, ec, si)] = prog.add(
                        "scalar", "dma_start",
                        out=xres_d[qc * 512 + si * 128: qc * 512 + (si + 1) * 128,
                                   ec * 512:(ec + 1) * 512],
                        in_=xt_s[si % 2],
                        waits=[(VE, ve_op_ev[(qc, ec, si)])], inc=SSX[si % 2], inck=16)
            if qc < 3:
                barrier()
        xres_w = [(SSX[0], SSX[0].n), (SSX[1], SSX[1].n)]
        barrier()
        ar.pop()
        ar.pop()

        if phases < 3:
            prog.add("scalar", None, waits=xres_w)
            prog.add("sync", None, waits=xres_w)
            barrier()
            _finish(nc, prog)
            return nc

        ar.push()
        acc = [ar.f32(2048) for _ in range(SCT)]
        xn2f = [ar.bf16(1024) for _ in range(DT)]
        h0_off = ar.alloc(0, align=512)
        h_bf = [ar.bf16(1024) for _ in range(IT)]
        trash_f32 = ar.f32_at(h0_off, 2048)
        wgu_ring = [ar.bf16(4096) for _ in range(2)]
        wdn_sb = [ar.bf16(2048) for _ in range(IT)]
        silu_buf = ar.f32_at(masks_off, 1024)

        prev_chunk_state = {}
        for chunk in range(NSC):
            r0 = chunk * 1024
            sx_acc = [0] * SCT
            ss_now = xres_w + [(SSY, SSY.n)]
            for si in range(SCT):
                sx_acc[si] = prog.add("sync", "dma_start", out=acc[si],
                                      in_=xres_d[r0 + si * 128: r0 + (si + 1) * 128, :],
                                      waits=(ss_now if si == 0 else []),
                                      inc=SXACC, inck=16)
            acc_w = [(SXACC, SXACC.n)]
            se_ln2 = [0] * SCT
            pe_tr2 = [0] * SCT

            def emit_tr2(si):
                dsts = [xn2f[t][:, si * 128:(si + 1) * 128] for t in range(DT)]
                _, pe_tr2[si] = emit_transpose16(xn_tok[si % 2], dsts, se_ln2[si])

            for si in range(SCT):
                prog.tag = f'E0.ln c={chunk} si={si}'
                dst_war = [(PE, pe_tr2[si - 2])] if si >= 2 else []
                se_ln2[si] = emit_ln(si % 2, acc[si], xn_tok[si % 2], trash_f32,
                                     acc_w, dst_war)
                if si >= 1:
                    emit_tr2(si - 1)
            emit_tr2(SCT - 1)
            xn2f_ready = max(tr_ev.values())
            ve_rt = 0
            pe_rt = [0] * SCT
            for si in range(SCT):
                prog.tag = f'E0.rt c={chunk} si={si}'
                first = True
                for t in range(DT):
                    w = []
                    if first:
                        w = [(VE, xn2f_ready), const_sw]
                        if si > 0:
                            w.append((VE, ve_rt))
                    pe_rt[si] = prog.add("tensor", "matmul", psF[:, 1024:1024 + 17],
                                         xn2f[t][:, si * 128:(si + 1) * 128],
                                         wrt_sb[:, t * 17:(t + 1) * 17],
                                         start=(t == 0), stop=(t == DT - 1),
                                         waits=w,
                                         inc=(PE if t == DT - 1 else None)) or pe_rt[si]
                    first = False
                prog.add("vector", "tensor_copy", rtb[:, si * 16:(si + 1) * 16],
                         psF[:, 1024:1024 + 16], waits=[(PE, pe_rt[si])])
                ve_rt = prog.add("vector", "tensor_copy", sgl[:, si:si + 1],
                                 psF[:, 1040:1041], inc=VE)
            se_sg = prog.add("scalar", "activation", sgl, sgl, AF.Sigmoid,
                             waits=[(VE, ve_rt)], inc=SE)
            ve_cmb = 0
            for si in range(SCT):
                m1 = stat_c(0, 16); m2 = stat_c(0, 17); m3 = stat_c(0, 18)
                m4 = stat_c(0, 19); msum = stat_c(0, 20); rcp = stat_c(0, 21)
                negm = stat_c(0, 22)
                rcol = rtb[:, si * 16:(si + 1) * 16]
                v1 = vsync("reduce_max", negm, rcol, axis=AX.X, negate=True)
                se_e = prog.add("scalar", "activation", ebuf, rcol, AF.Exp,
                                bias=negm, waits=[(VE, v1), (SE, se_sg)], inc=SE)
                vsync("reduce_max", m1, ebuf, axis=AX.X, waits=[(SE, se_e)])
                vsync("tensor_scalar", tbuf, ebuf, m1, None, ALU.is_equal)
                vsync("scalar_tensor_tensor", e1buf, tbuf, -2.0, ebuf,
                      ALU.mult, ALU.add)
                vsync("reduce_max", m2, e1buf, axis=AX.X)
                vsync("tensor_scalar", tbuf, e1buf, m2, None, ALU.is_equal)
                vsync("scalar_tensor_tensor", e1buf, tbuf, -2.0, e1buf,
                      ALU.mult, ALU.add)
                vsync("reduce_max", m3, e1buf, axis=AX.X)
                vsync("tensor_scalar", tbuf, e1buf, m3, None, ALU.is_equal)
                vsync("scalar_tensor_tensor", e1buf, tbuf, -2.0, e1buf,
                      ALU.mult, ALU.add)
                vsync("reduce_max", m4, e1buf, axis=AX.X)
                vsync("tensor_add", msum, m1, m2)
                vsync("tensor_add", msum, msum, m3)
                vsync("tensor_add", msum, msum, m4)
                vsync("reciprocal", rcp, msum)
                vsync("tensor_scalar", tbuf, ebuf, m4, None, ALU.is_ge)
                vsync("tensor_mul", tbuf, tbuf, ebuf)
                ve_cmb = vsync("tensor_scalar", cmb[:, si * 16:(si + 1) * 16],
                               tbuf, rcp, None, ALU.mult)

            pe_gu = {}
            ve_h = {}
            pe_dn = {}
            ve_acc = {}
            pe_dn_all = prev_chunk_state.get("pe_dn_all", 0)
            for u in range(units_limit):
                for j in range(IT):
                    g = u * IT + j
                    prog.tag = f'E1.gu c={chunk} u={u} j={j}'
                    w = []
                    if u == 0 and j == 0:
                        w.extend(xres_w)
                    gp = g - 2
                    if gp >= 0:
                        w.append((PE, pe_gu[gp]))
                    sw = prog.add("gpsimd", "dma_start", out=wgu_ring[g % 2],
                                  in_=wgu_d[u, j], waits=w, inc=SGU[g % 2], inck=16)
                    if j >= 2:
                        jjs = {9: (7, 8), 10: (9, 10)}.get(j, (j - 2,))
                        for jj in jjs:
                            sw_dn_last = prog.add(
                                "gpsimd", "dma_start", out=wdn_sb[jj],
                                in_=wdn_d[u, jj], inc=SWDN, inck=16)
                    half = (g % 2) * 4
                    w = [(SGU[g % 2], sw), (VE, xn2f_ready)]
                    if u == 0 and j == 0:
                        w.append((VE, ve_rt))
                    if gp >= 0 and ("h", gp) in ve_h:
                        w.append((VE, ve_h[("h", gp)]))
                    if u > 0 and j < 2:
                        w.append((VE, ve_acc[(u - 1, 6 + (g % 2))]))
                    first = True
                    for t in range(DT):
                        lg = wgu_ring[g % 2][:, t * 256:t * 256 + 128]
                        lu = wgu_ring[g % 2][:, t * 256 + 128:(t + 1) * 256]
                        for (lh, boff) in ((lg, 0), (lu, 2)):
                            for c in range(2):
                                pe_gu[g] = prog.add(
                                    "tensor", "matmul", PSF(half + boff + c),
                                    lh, xn2f[t][:, c * 512:(c + 1) * 512],
                                    start=(t == 0), stop=(t == DT - 1),
                                    waits=(w if first else []),
                                    inc=(PE if (t == DT - 1 and boff == 2 and c == 1)
                                         else None)) or pe_gu.get(g, 0)
                                first = False
                    w2 = [(PE, pe_gu[g])]
                    if ("h", g - 1) in ve_h:
                        w2.append((VE, ve_h[("h", g - 1)]))
                    se_si = prog.add("scalar", "activation", silu_buf,
                                     PSF(half, 2), AF.Silu, waits=w2, inc=SE)
                    ve_h[("h", g)] = prog.add("vector", "tensor_tensor", h_bf[j],
                                              silu_buf, PSF(half + 2, 2), ALU.mult,
                                              waits=[(SE, se_si)], inc=VE)
                gu10_half = ((u * IT + IT - 1) % 2) * 4
                si_order = list(range(SCT))
                if gu10_half == 0:
                    si_order = [1, 0, 3, 2, 5, 4, 7, 6]
                for k, si in enumerate(si_order):
                    prog.tag = f'E1.dn c={chunk} u={u} si={si}'
                    half = (si % 2) * 4
                    w0 = [(SWDN, sw_dn_last)]
                    w0.append((VE, ve_cmb) if u < 16 else (SE, se_sg))
                    if k == 0:
                        w0.append((VE, ve_h[("h", u * IT + IT - 2)]))
                    elif k == 1:
                        w0.append((VE, ve_h[("h", u * IT + IT - 1)]))
                    else:
                        w0.append((VE, ve_acc[(u, si_order[k - 2])]))
                    for j in range(IT):
                        wj = []
                        if k == 0 and j > 0:
                            wj = [(VE, ve_h[("h", u * IT + j)])]
                        if j == 0:
                            wj = w0 + [(VE, ve_h[("h", u * IT)])]
                        for hc in range(4):
                            pe_dn[(u, si)] = prog.add(
                                "tensor", "matmul", PSF(half + hc),
                                h_bf[j][:, si * 128:(si + 1) * 128],
                                wdn_sb[j][:, hc * 512:(hc + 1) * 512],
                                start=(j == 0), stop=(j == IT - 1),
                                waits=(wj if hc == 0 else []),
                                inc=(PE if (j == IT - 1 and hc == 3) else None)
                            ) or pe_dn.get((u, si), 0)
                    scal = (cmb[:, si * 16 + u: si * 16 + u + 1] if u < 16
                            else sgl[:, si:si + 1])
                    ve_acc[(u, si)] = prog.add(
                        "vector", "scalar_tensor_tensor", acc[si], PSF(half, 4),
                        scal, acc[si], ALU.mult, ALU.add,
                        waits=[(PE, pe_dn[(u, si)])], inc=VE)
                pe_dn_all = pe_dn[(u, si_order[SCT - 1])]
            prev_chunk_state["pe_dn_all"] = pe_dn_all
            for si in range(SCT):
                wy = ([(VE, ve_acc[(units_limit - 1, si)])] if units_limit > 0
                      else acc_w + [(VE, ve_cmb)])
                prog.add("scalar", "dma_start",
                         out=y_d[r0 + si * 128: r0 + (si + 1) * 128, :],
                         in_=acc[si], waits=wy, inc=SSY, inck=16)
            if chunk == 0:
                barrier()
        ar.pop()

        final_w = xres_w + [(SSY, SSY.n)]
        prog.add("scalar", None, waits=final_w)
        prog.add("sync", None, waits=final_w)
        barrier()
        _finish(nc, prog)
    return nc


def _finish(nc, prog):
    with nc.Block() as block:
        block.sync(lambda e: [f(e) for f in prog.q["sync"]])
        block.gpsimd(lambda e: [f(e) for f in prog.q["gpsimd"]])
        block.tensor(lambda e: [f(e) for f in prog.q["tensor"]])
        block.vector(lambda e: [f(e) for f in prog.q["vector"]])
        block.scalar(lambda e: [f(e) for f in prog.q["scalar"]])



def _prep_inputs(L0_attn_x, L0_attn_Wqkv_w, L0_attn_Wqkv_b, L0_attn_out_w,
                 L0_attn_out_b, L0_ffn_gate_w, L0_ffn_expert_gate_up,
                 L0_ffn_expert_down, L0_ffn_shared_gate_up, L0_ffn_shared_down,
                 L0_ffn_shared_gate_vec):
    f = np.ascontiguousarray

    def lhsT_tiles(w, n_tiles):
        out = np.empty((n_tiles, 128, 2048), dtype=np.float32)
        for j in range(n_tiles):
            out[j] = (w[j * 128:(j + 1) * 128, :].reshape(128, 16, 128)
                      .transpose(2, 1, 0).reshape(128, 2048))
        return out

    wqkv = lhsT_tiles(np.asarray(L0_attn_Wqkv_w, np.float32), ET)
    wqkvb = f(np.asarray(L0_attn_Wqkv_b, np.float32).reshape(ET, 128).T)
    wout = f(np.asarray(L0_attn_out_w, np.float32).T.reshape(DT, 128, 2048))
    woutb = f(np.asarray(L0_attn_out_b, np.float32).reshape(1, 2048))
    gw = np.asarray(L0_ffn_gate_w, np.float32)
    sgv = np.asarray(L0_ffn_shared_gate_vec, np.float32).reshape(2048)
    wrt = np.empty((128, DT * 17), dtype=np.float32)
    for t in range(DT):
        wrt[:, t * 17:t * 17 + 16] = gw[:, t * 128:(t + 1) * 128].T
        wrt[:, t * 17 + 16] = sgv[t * 128:(t + 1) * 128]
    egu = np.asarray(L0_ffn_expert_gate_up, np.float32)
    edn = np.asarray(L0_ffn_expert_down, np.float32)
    sgu = np.asarray(L0_ffn_shared_gate_up, np.float32)
    sdn = np.asarray(L0_ffn_shared_down, np.float32)
    wgu = np.empty((UNITS, IT, 128, 4096), dtype=np.float32)
    wdn = np.empty((UNITS, IT, 128, 2048), dtype=np.float32)

    def gu_img(ws):
        return ws.reshape(128, 16, 128).transpose(2, 1, 0)

    for u in range(UNITS):
        if u < 16:
            g_rows = egu[u, 0:1408]
            u_rows = egu[u, 1408:2816]
            dT = edn[u].T
        else:
            k = u - 16
            g_rows = sgu[k * 1408:(k + 1) * 1408]
            u_rows = sgu[5632 + k * 1408: 5632 + (k + 1) * 1408]
            dT = sdn.T[k * 1408:(k + 1) * 1408]
        for j in range(IT):
            gi = gu_img(g_rows[j * 128:(j + 1) * 128])
            ui = gu_img(u_rows[j * 128:(j + 1) * 128])
            wgu[u, j] = np.concatenate([gi, ui], axis=2).reshape(128, 4096)
            wdn[u, j] = dT[j * 128:(j + 1) * 128]

    ident = np.eye(128, dtype=ml_dtypes.bfloat16)
    ones = np.ones((1, 128), dtype=ml_dtypes.bfloat16)
    masks = np.empty((128, 4 * 512), dtype=ml_dtypes.bfloat16)
    p_i = np.arange(128)[:, None]
    c_i = np.arange(512)[None, :]
    for r in range(4):
        masks[:, r * 512:(r + 1) * 512] = np.where(
            c_i <= 128 * r + p_i, 0.0, NEGINF).astype(ml_dtypes.bfloat16)

    shared = dict(wqkv=wqkv, wqkvb=wqkvb, wout=wout, woutb=woutb, wrt=wrt,
                  wgu=wgu, wdn=wdn, ident=ident, ones=ones, masks=masks)
    x = np.asarray(L0_attn_x, np.float32)
    return [dict(shared, x=f(x[b])) for b in range(8)]


_NC_CACHE = {}


def kernel(**inputs):
    in_maps = _prep_inputs(**inputs)
    if "prog" not in _NC_CACHE:
        _NC_CACHE["prog"] = build_program(dbg=False)
    nc = _NC_CACHE["prog"]
    res = run_bass_kernel_spmd(nc, in_maps, core_ids=list(range(8)))
    return np.stack([res.results[b]["y"] for b in range(8)], axis=0)


# version 15
# speedup vs baseline: 1.1556x; 1.1556x over previous
import numpy as np
import ml_dtypes
from contextlib import ExitStack

import concourse.bass as bass
import concourse.mybir as mybir
from concourse.bass_utils import run_bass_kernel_spmd

F32 = mybir.dt.float32
BF16 = mybir.dt.bfloat16
U8 = mybir.dt.uint8
AF = mybir.ActivationFunctionType
ALU = mybir.AluOpType
AX = mybir.AxisListType

S = 2048
DM = 2048
NT = 16
DT = 16
HEADS = 16
ET = 18
IT = 11
UNITS = 20
NSC = 2
SCT = 8
QS = float(1.0 / np.sqrt(128.0))
EPS = 1e-5
NEGINF = -1e30
ARENA_B = 205312

ENGINES = ("sync", "gpsimd", "tensor", "vector", "scalar")


class Sem:
    def __init__(self, h):
        self.h = h
        self.n = 0


class Prog:
    def __init__(self):
        self.q = {e: [] for e in ENGINES}
        self.tag = "init"
        self.tagmap = {}

    def add(self, eng, meth, *args, waits=(), inc=None, inck=1, **kw):
        val = None
        if inc is not None:
            inc.n += inck
            val = inc.n
        wlist = [(s.h, v) for (s, v) in waits if v > 0]
        ih = inc.h if inc is not None else None
        tag = self.tag
        tm = self.tagmap

        def fn(e, meth=meth, args=args, kw=kw, wlist=wlist, ih=ih, inck=inck):
            for h, v in wlist:
                e.wait_ge(h, v)
            if meth is None:
                return
            ins = getattr(e, meth)(*args, **kw)
            if ih is not None:
                ins.then_inc(ih, inck)
            try:
                tm[ins.ins.name] = tag
            except AttributeError:
                try:
                    tm[ins.name] = tag
                except Exception:
                    pass
        self.q[eng].append(fn)
        return val


class Arena:
    def __init__(self, nc):
        self.h8 = nc.alloc_sbuf_tensor("arena", [128, ARENA_B], U8)
        self.hb = self.h8.bitcast(BF16).ap()
        self.hf = self.h8.bitcast(F32).ap()
        self.off = 0
        self.marks = []
        self.peak = 0

    def push(self):
        self.marks.append(self.off)

    def pop(self):
        self.off = self.marks.pop()

    def alloc(self, nbytes, align=512):
        o = (self.off + align - 1) // align * align
        self.off = o + nbytes
        self.peak = max(self.peak, self.off)
        assert self.off <= ARENA_B, f"arena overflow: {self.off} > {ARENA_B}"
        return o

    def f32(self, cols, align=512):
        o = self.alloc(cols * 4, align)
        return self.hf[:, o // 4:o // 4 + cols]

    def bf16(self, cols, align=512):
        o = self.alloc(cols * 2, align)
        return self.hb[:, o // 2:o // 2 + cols]

    def f32_at(self, byte_off, cols):
        return self.hf[:, byte_off // 4:byte_off // 4 + cols]


_LAST = {}


def build_program(dbg=False, phases=3, units_limit=UNITS):
    nc = bass.Bass("TRN2", target_bir_lowering=False, debug=False)

    x_d = nc.dram_tensor("x", [S, DM], F32, kind="ExternalInput").ap()
    wqkv_d = nc.dram_tensor("wqkv", [ET, 128, 2048], F32, kind="ExternalInput").ap()
    wqkvb_d = nc.dram_tensor("wqkvb", [128, ET], F32, kind="ExternalInput").ap()
    wout_d = nc.dram_tensor("wout", [DT, 128, 2048], F32, kind="ExternalInput").ap()
    woutb_d = nc.dram_tensor("woutb", [1, 2048], F32, kind="ExternalInput").ap()
    wrt_d = nc.dram_tensor("wrt", [128, DT * 17], F32, kind="ExternalInput").ap()
    wgu_d = nc.dram_tensor("wgu", [UNITS, IT, 128, 4096], F32, kind="ExternalInput").ap()
    wdn_d = nc.dram_tensor("wdn", [UNITS, IT, 128, 2048], F32, kind="ExternalInput").ap()
    ident_d = nc.dram_tensor("ident", [128, 128], BF16, kind="ExternalInput").ap()
    ones_d = nc.dram_tensor("ones", [1, 128], BF16, kind="ExternalInput").ap()
    masks_d = nc.dram_tensor("masks", [128, 4 * 512], BF16, kind="ExternalInput").ap()
    y_d = nc.dram_tensor("y", [S, DM], F32, kind="ExternalOutput").ap()
    xres_d = nc.dram_tensor("xres", [S, DM], F32,
                            kind="ExternalOutput" if dbg else "Internal").ap()
    if dbg:
        dqkv_d = nc.dram_tensor("dbg_qkvf", [ET * 128, 2048], BF16, kind="ExternalOutput").ap()
        dxn1_d = nc.dram_tensor("dbg_xn1f", [DT * 128, 2048], BF16, kind="ExternalOutput").ap()

    prog = Prog()
    _LAST['tagmap'] = prog.tagmap

    with ExitStack() as top:
        ar = Arena(nc)
        ps8h = top.enter_context(nc.psum_tensor("ps8", [128, 4096], F32))
        psF = ps8h.ap()
        psH = ps8h.bitcast(BF16).ap()

        def PSB(b):
            return psH[:, b * 1024: b * 1024 + 128]

        def PSF(b, n=1):
            return psF[:, b * 512:(b + n) * 512]

        def mksem(name):
            return Sem(top.enter_context(nc.semaphore(name)))
        SXC = mksem("SXC")
        SXT = [mksem(f"SXT{i}") for i in range(2)]
        SXE = [mksem(f"SXE{i}") for i in range(2)]
        SXACC = mksem("SXACC")
        SXD = mksem("SXD")
        SWC = mksem("SWC")
        SWQ = [mksem(f"SWQ{i}") for i in range(4)]
        SOW = [mksem(f"SOW{i}") for i in range(4)]
        SGU = [mksem(f"SGU{i}") for i in range(2)]
        SWDN = mksem("SWDN")
        SSX = [mksem(f"SSX{i}") for i in range(2)]
        SSY = mksem("SSY")
        PE = mksem("PE")
        VE = mksem("VE")
        SE = mksem("SE")
        BAR = mksem("BAR")

        ident_sb = ar.bf16(128)
        masks_off = ar.alloc(4096)
        masks_sb = ar.hb[:, masks_off // 2: masks_off // 2 + 2048]
        qkvb_sb = ar.f32(ET, align=64)
        wrt_sb = ar.bf16(DT * 17, align=64)
        rowa = ar.alloc(4096 + 256)
        woutb_row = ar.hb[0:1, rowa // 2: rowa // 2 + 2048]
        ones_row = ar.hb[0:1, rowa // 2 + 2048: rowa // 2 + 2048 + 128]
        stat = ar.f32(64, align=64)
        cmb = ar.f32(SCT * 16, align=64)
        sgl = ar.f32(SCT, align=64)
        rtb = ar.f32(SCT * 16, align=64)
        ebuf = ar.f32(16, align=64)
        e1buf = ar.f32(16, align=64)
        tbuf = ar.f32(16, align=64)
        xn_tok = [ar.bf16(2048) for _ in range(2)]

        def stat_c(par, k, n=1):
            c = par * 32 + k
            return stat[:, c:c + n]

        prog.add("sync", "dma_start", out=ident_sb, in_=ident_d, inc=SXC, inck=16)
        prog.add("sync", "dma_start", out=masks_sb, in_=masks_d, inc=SXC, inck=16)
        prog.add("sync", "dma_start", out=qkvb_sb, in_=wqkvb_d, inc=SXC, inck=16)
        const_sx = (SXC, SXC.n)
        prog.add("gpsimd", "dma_start", out=wrt_sb, in_=wrt_d, inc=SWC, inck=16)
        prog.add("gpsimd", "dma_start", out=woutb_row, in_=woutb_d, inc=SWC, inck=16)
        prog.add("gpsimd", "dma_start", out=ones_row, in_=ones_d, inc=SWC, inck=16)
        const_sw = (SWC, SWC.n)

        def barrier():
            tgt = BAR.n + 5
            for e in ENGINES:
                prog.add(e, "nop", inc=BAR)
            BAR.n = tgt
            for e in ENGINES:
                prog.add(e, None, waits=[(BAR, tgt)])

        tr_ev = {0: 0, 1: 0}

        def vsync(meth, *args, waits=(), **kw):
            return prog.add("vector", meth, *args,
                            waits=[(VE, VE.n)] + list(waits), inc=VE, **kw)

        def emit_ln(par, src_f32, dst_bf, trash_f32, wait_src, dst_war=()):
            s1 = stat_c(par, 0); s2 = stat_c(par, 1); mcol = stat_c(par, 2)
            msq = stat_c(par, 3); var = stat_c(par, 4); rvar = stat_c(par, 5)
            rstd = stat_c(par, 6); negmr = stat_c(par, 7)
            n1 = prog.add("vector", "reduce_sum", s1, src_f32, axis=AX.X,
                          waits=wait_src, inc=VE)
            se_s2 = prog.add("scalar", "activation", trash_f32, src_f32, AF.Square,
                             accum_out=s2, waits=wait_src, inc=SE)
            n2 = prog.add("vector", "tensor_scalar_mul", mcol, s1, 1.0 / DM,
                          waits=[(SE, se_s2), (VE, n1)], inc=VE)
            n3 = prog.add("vector", "tensor_mul", msq, mcol, mcol,
                          waits=[(VE, n2)], inc=VE)
            n4 = prog.add("vector", "scalar_tensor_tensor", var, s2, 1.0 / DM, msq,
                          ALU.mult, ALU.subtract, waits=[(VE, n3)], inc=VE)
            n5 = prog.add("vector", "tensor_scalar_add", var, var, EPS,
                          waits=[(VE, n4)], inc=VE)
            n6 = prog.add("vector", "reciprocal", rvar, var, waits=[(VE, n5)], inc=VE)
            se_rs = prog.add("scalar", "activation", rstd, rvar, AF.Sqrt,
                             waits=[(VE, n6)], inc=SE)
            n7 = prog.add("vector", "scalar_tensor_tensor", negmr, mcol, -1.0,
                          rstd, ALU.mult, ALU.mult,
                          waits=[(SE, se_rs), (VE, n6)], inc=VE)
            return prog.add("scalar", "activation", dst_bf, src_f32, AF.Identity,
                            bias=negmr, scale=rstd,
                            waits=[(VE, n7)] + list(dst_war), inc=SE)

        def emit_transpose16(src_bf, dst_tiles, se_ready):
            last_pe = 0
            for t in range(DT):
                b = t % 2
                w = [(VE, tr_ev[b])]
                if t == 0:
                    w.append((SE, se_ready))
                last_pe = prog.add("tensor", "transpose", PSB(b),
                                   src_bf[:, t * 128:(t + 1) * 128], ident_sb,
                                   waits=w, inc=PE)
                tr_ev[b] = prog.add("vector", "tensor_copy", dst_tiles[t], PSB(b),
                                    waits=[(PE, last_pe)], inc=VE)
            return max(tr_ev[0], tr_ev[1]), last_pe

        ar.push()
        qkvf = [ar.bf16(2048) for _ in range(ET)]
        v_tok = ar.bf16(2048)
        ar.push()
        xn1f = [ar.bf16(2048) for _ in range(DT)]
        wq_ring = [ar.bf16(2048) for _ in range(4)]
        sq_trash = ar.f32(2048)
        xt_ring = [ar.f32(2048) for _ in range(2)]

        se_lnA = [0] * NT
        pe_trA = [0] * NT
        xa_done = [0] * NT
        def emit_trA(i):
            dsts = [xn1f[t][:, i * 128:(i + 1) * 128] for t in range(DT)]
            xa_done[i], pe_trA[i] = emit_transpose16(xn_tok[i % 2], dsts, se_lnA[i])
            if i > 0:
                xa_done[i] = max(xa_done[i], xa_done[i - 1])

        for i in range(NT):
            prog.tag = f'A.ln i={i}'
            w = [const_sx] if i < 2 else [(SE, se_lnA[i - 2])]
            sx = prog.add("sync", "dma_start", out=xt_ring[i % 2],
                          in_=x_d[i * 128:(i + 1) * 128, :], waits=w,
                          inc=SXT[i % 2], inck=16)
            dst_war = [(PE, pe_trA[i - 2])] if i >= 2 else []
            se_lnA[i] = emit_ln(i % 2, xt_ring[i % 2], xn_tok[i % 2], sq_trash,
                                [(SXT[i % 2], sx)], dst_war)
            if i >= 1:
                emit_trA(i - 1)
        emit_trA(NT - 1)
        xn1f_ready = max(tr_ev.values())

        sw_chunk = [0] * ET
        pe_grp = [0] * ET
        se_evB = [0] * ET
        for j in range(ET):
            prog.tag = f'B.qkv j={j}'
            w = [(PE, pe_grp[j - 4])] if j >= 4 else []
            sw_chunk[j] = prog.add("gpsimd", "dma_start", out=wq_ring[j % 4],
                                   in_=wqkv_d[j], waits=w, inc=SWQ[j % 4], inck=16)
            half = (j % 2) * 4
            for c in range(4):
                w = [(SWQ[j % 4], sw_chunk[j]), (VE, xa_done[4 * c + 3])]
                if j >= 2:
                    w.append((SE, se_evB[j - 2]))
                for t in range(DT):
                    pe_grp[j] = prog.add(
                        "tensor", "matmul", PSF(half + c),
                        wq_ring[j % 4][:, t * 128:(t + 1) * 128],
                        xn1f[t][:, c * 512:(c + 1) * 512],
                        start=(t == 0), stop=(t == DT - 1),
                        waits=(w if t == 0 else []),
                        inc=(PE if (t == DT - 1 and c == 3) else None)) or pe_grp[j]
            se_evB[j] = prog.add("scalar", "activation", qkvf[j], PSF(half, 4),
                                 AF.Identity, bias=qkvb_sb[:, j:j + 1],
                                 waits=[(PE, pe_grp[j])], inc=SE)
        dsts = [v_tok[:, t * 128:(t + 1) * 128] for t in range(DT)]
        vtok_ready, _ = emit_transpose16(qkvf[17], dsts, se_evB[17])

        if dbg:
            for j in range(ET):
                prog.add("sync", "dma_start", out=dqkv_d[j * 128:(j + 1) * 128, :],
                         in_=qkvf[j], waits=[(SE, se_evB[j]), (VE, vtok_ready)],
                         inc=SXD, inck=16)
            for t in range(DT):
                prog.add("sync", "dma_start", out=dxn1_d[t * 128:(t + 1) * 128, :],
                         in_=xn1f[t], waits=[(VE, xn1f_ready)], inc=SXD, inck=16)

        barrier()
        ar.pop()

        if phases < 2:
            prog.add("sync", None, waits=[(SXD, SXD.n)])
            barrier()
            _finish(nc, prog)
            return nc
        ar.push()
        ctxf = [ar.bf16(512) for _ in range(DT)]
        pT_all = ar.bf16(NT * 512)
        pT = [pT_all[:, kt * 512:(kt + 1) * 512] for kt in range(NT)]
        pT3 = pT_all.rearrange("p (kt c) -> p kt c", kt=NT)
        outw_ring = [ar.bf16(8 * 512) for _ in range(4)]
        p_sb = [ar.bf16(2048) for _ in range(4)]
        row_sb = [ar.f32(2048) for _ in range(2)]
        diag = [ar.bf16(128, align=64) for _ in range(4)]
        cstat = ar.f32(64, align=64)
        xt_s = [ar.f32(512) for _ in range(2)]

        se_exp = {}
        ve_pt = {}
        pe_pt = {}
        pe_ctx = {}
        ve_ctx_ev = {}
        ve_ttr = {}
        se_ttr = {}
        pe_op_grp = {}
        ve_op_ev = {}
        ss_xres = {}

        for qc in range(4):
            n_kt = 4 * (qc + 1)
            nkc = qc + 1
            L = nkc * 512
            tiles = [(h, ql) for h in range(HEADS) for ql in range(4)]
            bank_ev = {4: 0, 5: 0}
            ve_red = {}

            def warm():
                prog.add("tensor", "matmul", psF[:, 7 * 512:7 * 512 + 64],
                         ident_sb, ident_sb[:, 0:64], start=True, stop=True)

            def emit_sc(idx):
                h, ql = tiles[idx]
                prog.tag = f'C.head qc={qc} h={h} ql={ql}'
                qi = qc * 4 + ql
                w = [(VE, vtok_ready)]
                if idx > 0:
                    w.append((SE, se_ttr[tiles[idx - 1]]))
                pe_sc = 0
                for kc in range(nkc):
                    pe_sc = prog.add("tensor", "matmul", PSF(kc),
                                     qkvf[h][:, qi * 128:(qi + 1) * 128],
                                     qkvf[16][:, kc * 512:(kc + 1) * 512],
                                     start=True, stop=(kc != qc),
                                     waits=(w if kc == 0 else []),
                                     inc=(None if kc == qc else PE))
                pe_sc = prog.add("tensor", "matmul", PSF(qc), ident_sb,
                                 masks_sb[:, (qi % 4) * 512:(qi % 4 + 1) * 512],
                                 start=False, stop=True, inc=PE)
                return pe_sc

            def emit_evac(idx, pe_sc):
                h, ql = tiles[idx]
                prog.tag = f'C.head qc={qc} h={h} ql={ql}'
                par = ql % 2
                se_ttr[(h, ql)] = prog.add(
                    "scalar", "activation", row_sb[par][:, 0:L], psF[:, 0:L],
                    AF.Identity, scale=QS, waits=[(PE, pe_sc)], inc=SE)

            def emit_red(idx):
                h, ql = tiles[idx]
                prog.tag = f'C.head qc={qc} h={h} ql={ql}'
                par = ql % 2
                negm = cstat[:, ql * 16 + 9:ql * 16 + 10]
                ve_red[(h, ql)] = prog.add(
                    "vector", "reduce_max", negm, row_sb[par][:, 0:L],
                    axis=AX.X, negate=True,
                    waits=[(SE, se_ttr[(h, ql)])], inc=VE)

            def emit_exp(idx):
                h, ql = tiles[idx]
                prog.tag = f'C.head qc={qc} h={h} ql={ql}'
                par = ql % 2
                negm = cstat[:, ql * 16 + 9:ql * 16 + 10]
                w = [(VE, ve_red[(h, ql)])]
                if idx >= 4 and tiles[idx - 4] in pe_pt:
                    w.append((PE, pe_pt[tiles[idx - 4]]))
                se_exp[(h, ql)] = prog.add(
                    "scalar", "activation", p_sb[ql][:, 0:L],
                    row_sb[par][:, 0:L], AF.Exp, bias=negm,
                    accum_out=cstat[:, ql * 16 + 10:ql * 16 + 11], waits=w, inc=SE)

            def emit_back(idx):
                h, ql = tiles[idx]
                prog.tag = f'C.pt qc={qc} h={h} ql={ql}'
                sume = cstat[:, ql * 16 + 10:ql * 16 + 11]
                rs = cstat[:, ql * 16 + 12:ql * 16 + 13]
                ve_rs = prog.add("vector", "reciprocal", rs, sume,
                                 waits=[(SE, se_exp[(h, ql)])], inc=VE)
                wdg = [(VE, ve_rs)]
                if idx >= 4 and tiles[idx - 4] in pe_pt:
                    wdg.append((PE, pe_pt[tiles[idx - 4]]))
                ve_dg = prog.add("vector", "tensor_scalar", diag[ql], ident_sb,
                                 rs, None, ALU.mult, waits=wdg, inc=VE)
                warm()
                for kt in range(n_kt):
                    grp = kt // 4
                    b = 4 + (grp % 2)
                    w2 = [(VE, ve_dg)] if kt == 0 else []
                    if kt % 4 == 0 and bank_ev[b] > 0:
                        w2.append((VE, bank_ev[b]))
                    pe_pt[(h, ql)] = prog.add(
                        "tensor", "matmul",
                        psF[:, b * 512 + (kt % 4) * 128: b * 512 + (kt % 4 + 1) * 128],
                        p_sb[ql][:, kt * 128:(kt + 1) * 128], diag[ql],
                        start=True, stop=True, waits=w2, inc=PE)
                    if kt % 4 == 3 or kt == n_kt - 1:
                        nk = kt % 4 + 1
                        bank_ev[b] = prog.add(
                            "vector", "tensor_copy",
                            pT3[:, grp * 4:grp * 4 + nk, ql * 128:(ql + 1) * 128],
                            psF[:, b * 512:b * 512 + nk * 128]
                            .rearrange("p (kt c) -> p kt c", kt=nk),
                            waits=[(PE, pe_pt[(h, ql)])], inc=VE)
                if ql == 3:
                    prog.tag = f'C.ctx qc={qc} h={h}'
                    cb = 6
                    w = [(VE, VE.n)]
                    if (qc, h - 1) in ve_ctx_ev:
                        w.append((VE, ve_ctx_ev[(qc, h - 1)]))
                    for kt in range(n_kt):
                        pe_ctx[(qc, h)] = prog.add(
                            "tensor", "matmul", PSF(cb),
                            v_tok[:, kt * 128:(kt + 1) * 128], pT[kt],
                            start=(kt == 0), stop=(kt == n_kt - 1),
                            waits=(w if kt == 0 else []),
                            inc=(PE if kt == n_kt - 1 else None)) or pe_ctx.get((qc, h), 0)
                    ve_ctx_ev[(qc, h)] = prog.add(
                        "vector", "tensor_copy", ctxf[h], PSF(cb),
                        waits=[(PE, pe_ctx[(qc, h)])], inc=VE)

            NTL = len(tiles)
            for idx in range(NTL):
                pe_sc = emit_sc(idx)
                emit_evac(idx, pe_sc)
                if idx >= 2:
                    emit_back(idx - 2)
                emit_red(idx)
                if idx >= 1:
                    emit_exp(idx - 1)
            emit_exp(NTL - 1)
            emit_back(NTL - 2)
            emit_back(NTL - 1)
            ctx_ready = max(ve_ctx_ev[(qc, hh)] for hh in range(HEADS))
            for ec in range(4):
                prog.tag = f'C.op qc={qc} ec={ec}'
                sw_half = [0, 0]
                for half in range(2):
                    kk = (qc * 4 + ec) * 2 + half
                    w = []
                    kprev = (qc, ec - 2) if ec >= 2 else ((qc - 1, ec + 2) if qc > 0 else None)
                    if kprev is not None and kprev in pe_op_grp:
                        w.append((PE, pe_op_grp[kprev]))
                    if kk == 0:
                        w.append((SXD, SXD.n))
                    sw_half[half] = prog.add(
                        "gpsimd", "dma_start",
                        out=outw_ring[kk % 4].rearrange("p (t c) -> p t c", t=8),
                        in_=wout_d[half * 8:(half + 1) * 8, :,
                                   ec * 512:(ec + 1) * 512].transpose([1, 0, 2]),
                        waits=w, inc=SOW[kk % 4], inck=16)
                for si in range(4):
                    bank = si % 4
                    wx = []
                    kp = (qc, ec, si - 2) if si >= 2 else (
                        (qc, ec - 1, si + 2) if ec > 0 else
                        ((qc - 1, 3, si + 2) if qc > 0 else None))
                    if kp is not None and kp in ss_xres:
                        wx.append((SSX[si % 2], ss_xres[kp]))
                        wx.append((VE, ve_op_ev[kp]))
                    sx_x = prog.add("sync", "dma_start", out=xt_s[si % 2],
                                    in_=x_d[qc * 512 + si * 128:
                                            qc * 512 + (si + 1) * 128,
                                            ec * 512:(ec + 1) * 512],
                                    waits=wx, inc=SXE[si % 2], inck=16)
                    w = []
                    if si == 0:
                        w = [(SOW[((qc * 4 + ec) * 2) % 4], sw_half[0]), (VE, ctx_ready)]
                        if ec == 0 and qc == 0:
                            w.append(const_sw)
                        if ec > 0:
                            w.append((VE, ve_op_ev[(qc, ec - 1, 3)]))
                        elif qc > 0:
                            w.append((VE, ve_op_ev[(qc - 1, 3, 3)]))
                    prog.add("tensor", "matmul", PSF(bank), ones_row,
                             woutb_row[:, ec * 512:(ec + 1) * 512],
                             start=True, stop=False, waits=w)
                    for t in range(DT):
                        tw = []
                        if t == 8:
                            tw = [(SOW[(((qc * 4 + ec) * 2) + 1) % 4], sw_half[1])]
                        pe_op_grp[(qc, ec)] = prog.add(
                            "tensor", "matmul", PSF(bank),
                            ctxf[t][:, si * 128:(si + 1) * 128],
                            outw_ring[(((qc * 4 + ec) * 2) + t // 8) % 4][
                                :, (t % 8) * 512:(t % 8 + 1) * 512],
                            start=False, stop=(t == DT - 1), waits=tw,
                            inc=(PE if t == DT - 1 else None)) or pe_op_grp.get((qc, ec), 0)
                    ve_op_ev[(qc, ec, si)] = prog.add(
                        "vector", "tensor_tensor", xt_s[si % 2], PSF(bank),
                        xt_s[si % 2], ALU.add,
                        waits=[(PE, pe_op_grp[(qc, ec)]), (SXE[si % 2], sx_x)], inc=VE)
                    ss_xres[(qc, ec, si)] = prog.add(
                        "scalar", "dma_start",
                        out=xres_d[qc * 512 + si * 128: qc * 512 + (si + 1) * 128,
                                   ec * 512:(ec + 1) * 512],
                        in_=xt_s[si % 2],
                        waits=[(VE, ve_op_ev[(qc, ec, si)])], inc=SSX[si % 2], inck=16)
            if qc < 3:
                barrier()
        xres_w = [(SSX[0], SSX[0].n), (SSX[1], SSX[1].n)]
        barrier()
        ar.pop()
        ar.pop()

        if phases < 3:
            prog.add("scalar", None, waits=xres_w)
            prog.add("sync", None, waits=xres_w)
            barrier()
            _finish(nc, prog)
            return nc

        ar.push()
        acc = [ar.f32(2048) for _ in range(SCT)]
        xn2f = [ar.bf16(1024) for _ in range(DT)]
        h0_off = ar.alloc(0, align=512)
        h_bf = [ar.bf16(1024) for _ in range(IT)]
        trash_f32 = ar.f32_at(h0_off, 2048)
        wgu_ring = [ar.bf16(4096) for _ in range(2)]
        wdn_sb = [ar.bf16(2048) for _ in range(IT)]
        silu_buf = ar.f32_at(masks_off, 1024)

        prev_chunk_state = {}
        for chunk in range(NSC):
            r0 = chunk * 1024
            sx_acc = [0] * SCT
            ss_now = xres_w + [(SSY, SSY.n)]
            for si in range(SCT):
                sx_acc[si] = prog.add("sync", "dma_start", out=acc[si],
                                      in_=xres_d[r0 + si * 128: r0 + (si + 1) * 128, :],
                                      waits=(ss_now if si == 0 else []),
                                      inc=SXACC, inck=16)
            acc_w = [(SXACC, SXACC.n)]
            se_ln2 = [0] * SCT
            pe_tr2 = [0] * SCT

            def emit_tr2(si):
                dsts = [xn2f[t][:, si * 128:(si + 1) * 128] for t in range(DT)]
                _, pe_tr2[si] = emit_transpose16(xn_tok[si % 2], dsts, se_ln2[si])

            for si in range(SCT):
                prog.tag = f'E0.ln c={chunk} si={si}'
                dst_war = [(PE, pe_tr2[si - 2])] if si >= 2 else []
                se_ln2[si] = emit_ln(si % 2, acc[si], xn_tok[si % 2], trash_f32,
                                     acc_w, dst_war)
                if si >= 1:
                    emit_tr2(si - 1)
            emit_tr2(SCT - 1)
            xn2f_ready = max(tr_ev.values())
            ve_rt = 0
            pe_rt = [0] * SCT
            for si in range(SCT):
                prog.tag = f'E0.rt c={chunk} si={si}'
                first = True
                for t in range(DT):
                    w = []
                    if first:
                        w = [(VE, xn2f_ready), const_sw]
                        if si > 0:
                            w.append((VE, ve_rt))
                    pe_rt[si] = prog.add("tensor", "matmul", psF[:, 1024:1024 + 17],
                                         xn2f[t][:, si * 128:(si + 1) * 128],
                                         wrt_sb[:, t * 17:(t + 1) * 17],
                                         start=(t == 0), stop=(t == DT - 1),
                                         waits=w,
                                         inc=(PE if t == DT - 1 else None)) or pe_rt[si]
                    first = False
                prog.add("vector", "tensor_copy", rtb[:, si * 16:(si + 1) * 16],
                         psF[:, 1024:1024 + 16], waits=[(PE, pe_rt[si])])
                ve_rt = prog.add("vector", "tensor_copy", sgl[:, si:si + 1],
                                 psF[:, 1040:1041], inc=VE)
            se_sg = prog.add("scalar", "activation", sgl, sgl, AF.Sigmoid,
                             waits=[(VE, ve_rt)], inc=SE)
            ve_cmb = 0
            for si in range(SCT):
                m1 = stat_c(0, 16); m2 = stat_c(0, 17); m3 = stat_c(0, 18)
                m4 = stat_c(0, 19); msum = stat_c(0, 20); rcp = stat_c(0, 21)
                negm = stat_c(0, 22)
                rcol = rtb[:, si * 16:(si + 1) * 16]
                v1 = vsync("reduce_max", negm, rcol, axis=AX.X, negate=True)
                se_e = prog.add("scalar", "activation", ebuf, rcol, AF.Exp,
                                bias=negm, waits=[(VE, v1), (SE, se_sg)], inc=SE)
                vsync("reduce_max", m1, ebuf, axis=AX.X, waits=[(SE, se_e)])
                vsync("tensor_scalar", tbuf, ebuf, m1, None, ALU.is_equal)
                vsync("scalar_tensor_tensor", e1buf, tbuf, -2.0, ebuf,
                      ALU.mult, ALU.add)
                vsync("reduce_max", m2, e1buf, axis=AX.X)
                vsync("tensor_scalar", tbuf, e1buf, m2, None, ALU.is_equal)
                vsync("scalar_tensor_tensor", e1buf, tbuf, -2.0, e1buf,
                      ALU.mult, ALU.add)
                vsync("reduce_max", m3, e1buf, axis=AX.X)
                vsync("tensor_scalar", tbuf, e1buf, m3, None, ALU.is_equal)
                vsync("scalar_tensor_tensor", e1buf, tbuf, -2.0, e1buf,
                      ALU.mult, ALU.add)
                vsync("reduce_max", m4, e1buf, axis=AX.X)
                vsync("tensor_add", msum, m1, m2)
                vsync("tensor_add", msum, msum, m3)
                vsync("tensor_add", msum, msum, m4)
                vsync("reciprocal", rcp, msum)
                vsync("tensor_scalar", tbuf, ebuf, m4, None, ALU.is_ge)
                vsync("tensor_mul", tbuf, tbuf, ebuf)
                ve_cmb = vsync("tensor_scalar", cmb[:, si * 16:(si + 1) * 16],
                               tbuf, rcp, None, ALU.mult)

            pe_gu = {}
            ve_h = {}
            pe_dn = {}
            ve_acc = {}
            pe_dn_all = prev_chunk_state.get("pe_dn_all", 0)
            for u in range(units_limit):
                for j in range(IT):
                    g = u * IT + j
                    prog.tag = f'E1.gu c={chunk} u={u} j={j}'
                    w = []
                    if u == 0 and j == 0:
                        w.extend(xres_w)
                    gp = g - 2
                    if gp >= 0:
                        w.append((PE, pe_gu[gp]))
                    sw = prog.add("gpsimd", "dma_start", out=wgu_ring[g % 2],
                                  in_=wgu_d[u, j], waits=w, inc=SGU[g % 2], inck=16)
                    if j >= 2:
                        jjs = {9: (7, 8), 10: (9, 10)}.get(j, (j - 2,))
                        for jj in jjs:
                            sw_dn_last = prog.add(
                                "gpsimd", "dma_start", out=wdn_sb[jj],
                                in_=wdn_d[u, jj], inc=SWDN, inck=16)
                    half = (g % 2) * 4
                    w = [(SGU[g % 2], sw), (VE, xn2f_ready)]
                    if u == 0 and j == 0:
                        w.append((VE, ve_rt))
                    if gp >= 0 and ("h", gp) in ve_h:
                        w.append((VE, ve_h[("h", gp)]))
                    if u > 0 and j < 2:
                        w.append((VE, ve_acc[(u - 1, 6 + (g % 2))]))
                    first = True
                    for t in range(DT):
                        lg = wgu_ring[g % 2][:, t * 256:t * 256 + 128]
                        lu = wgu_ring[g % 2][:, t * 256 + 128:(t + 1) * 256]
                        for (lh, boff) in ((lg, 0), (lu, 2)):
                            for c in range(2):
                                pe_gu[g] = prog.add(
                                    "tensor", "matmul", PSF(half + boff + c),
                                    lh, xn2f[t][:, c * 512:(c + 1) * 512],
                                    start=(t == 0), stop=(t == DT - 1),
                                    waits=(w if first else []),
                                    inc=(PE if (t == DT - 1 and boff == 2 and c == 1)
                                         else None)) or pe_gu.get(g, 0)
                                first = False
                    w2 = [(PE, pe_gu[g])]
                    if ("h", g - 1) in ve_h:
                        w2.append((VE, ve_h[("h", g - 1)]))
                    se_si = prog.add("scalar", "activation", silu_buf,
                                     PSF(half, 2), AF.Silu, waits=w2, inc=SE)
                    ve_h[("h", g)] = prog.add("vector", "tensor_tensor", h_bf[j],
                                              silu_buf, PSF(half + 2, 2), ALU.mult,
                                              waits=[(SE, se_si)], inc=VE)
                gu10_half = ((u * IT + IT - 1) % 2) * 4
                si_order = list(range(SCT))
                if gu10_half == 0:
                    si_order = [1, 0, 3, 2, 5, 4, 7, 6]
                for k, si in enumerate(si_order):
                    prog.tag = f'E1.dn c={chunk} u={u} si={si}'
                    half = (si % 2) * 4
                    w0 = [(SWDN, sw_dn_last)]
                    w0.append((VE, ve_cmb) if u < 16 else (SE, se_sg))
                    if k == 0:
                        w0.append((VE, ve_h[("h", u * IT + IT - 2)]))
                    elif k == 1:
                        w0.append((VE, ve_h[("h", u * IT + IT - 1)]))
                    else:
                        w0.append((VE, ve_acc[(u, si_order[k - 2])]))
                    for j in range(IT):
                        wj = []
                        if k == 0 and j > 0:
                            wj = [(VE, ve_h[("h", u * IT + j)])]
                        if j == 0:
                            wj = w0 + [(VE, ve_h[("h", u * IT)])]
                        for hc in range(4):
                            pe_dn[(u, si)] = prog.add(
                                "tensor", "matmul", PSF(half + hc),
                                h_bf[j][:, si * 128:(si + 1) * 128],
                                wdn_sb[j][:, hc * 512:(hc + 1) * 512],
                                start=(j == 0), stop=(j == IT - 1),
                                waits=(wj if hc == 0 else []),
                                inc=(PE if (j == IT - 1 and hc == 3) else None)
                            ) or pe_dn.get((u, si), 0)
                    scal = (cmb[:, si * 16 + u: si * 16 + u + 1] if u < 16
                            else sgl[:, si:si + 1])
                    ve_acc[(u, si)] = prog.add(
                        "vector", "scalar_tensor_tensor", acc[si], PSF(half, 4),
                        scal, acc[si], ALU.mult, ALU.add,
                        waits=[(PE, pe_dn[(u, si)])], inc=VE)
                pe_dn_all = pe_dn[(u, si_order[SCT - 1])]
            prev_chunk_state["pe_dn_all"] = pe_dn_all
            for si in range(SCT):
                wy = ([(VE, ve_acc[(units_limit - 1, si)])] if units_limit > 0
                      else acc_w + [(VE, ve_cmb)])
                prog.add("scalar", "dma_start",
                         out=y_d[r0 + si * 128: r0 + (si + 1) * 128, :],
                         in_=acc[si], waits=wy, inc=SSY, inck=16)
            if chunk == 0:
                barrier()
        ar.pop()

        final_w = xres_w + [(SSY, SSY.n)]
        prog.add("scalar", None, waits=final_w)
        prog.add("sync", None, waits=final_w)
        barrier()
        _finish(nc, prog)
    return nc


def _finish(nc, prog):
    with nc.Block() as block:
        block.sync(lambda e: [f(e) for f in prog.q["sync"]])
        block.gpsimd(lambda e: [f(e) for f in prog.q["gpsimd"]])
        block.tensor(lambda e: [f(e) for f in prog.q["tensor"]])
        block.vector(lambda e: [f(e) for f in prog.q["vector"]])
        block.scalar(lambda e: [f(e) for f in prog.q["scalar"]])



def _prep_inputs(L0_attn_x, L0_attn_Wqkv_w, L0_attn_Wqkv_b, L0_attn_out_w,
                 L0_attn_out_b, L0_ffn_gate_w, L0_ffn_expert_gate_up,
                 L0_ffn_expert_down, L0_ffn_shared_gate_up, L0_ffn_shared_down,
                 L0_ffn_shared_gate_vec):
    f = np.ascontiguousarray

    def lhsT_tiles(w, n_tiles):
        out = np.empty((n_tiles, 128, 2048), dtype=np.float32)
        for j in range(n_tiles):
            out[j] = (w[j * 128:(j + 1) * 128, :].reshape(128, 16, 128)
                      .transpose(2, 1, 0).reshape(128, 2048))
        return out

    wqkv = lhsT_tiles(np.asarray(L0_attn_Wqkv_w, np.float32), ET)
    wqkvb = f(np.asarray(L0_attn_Wqkv_b, np.float32).reshape(ET, 128).T)
    wout = f(np.asarray(L0_attn_out_w, np.float32).T.reshape(DT, 128, 2048))
    woutb = f(np.asarray(L0_attn_out_b, np.float32).reshape(1, 2048))
    gw = np.asarray(L0_ffn_gate_w, np.float32)
    sgv = np.asarray(L0_ffn_shared_gate_vec, np.float32).reshape(2048)
    wrt = np.empty((128, DT * 17), dtype=np.float32)
    for t in range(DT):
        wrt[:, t * 17:t * 17 + 16] = gw[:, t * 128:(t + 1) * 128].T
        wrt[:, t * 17 + 16] = sgv[t * 128:(t + 1) * 128]
    egu = np.asarray(L0_ffn_expert_gate_up, np.float32)
    edn = np.asarray(L0_ffn_expert_down, np.float32)
    sgu = np.asarray(L0_ffn_shared_gate_up, np.float32)
    sdn = np.asarray(L0_ffn_shared_down, np.float32)
    wgu = np.empty((UNITS, IT, 128, 4096), dtype=np.float32)
    wdn = np.empty((UNITS, IT, 128, 2048), dtype=np.float32)

    def gu_img(ws):
        return ws.reshape(128, 16, 128).transpose(2, 1, 0)

    for u in range(UNITS):
        if u < 16:
            g_rows = egu[u, 0:1408]
            u_rows = egu[u, 1408:2816]
            dT = edn[u].T
        else:
            k = u - 16
            g_rows = sgu[k * 1408:(k + 1) * 1408]
            u_rows = sgu[5632 + k * 1408: 5632 + (k + 1) * 1408]
            dT = sdn.T[k * 1408:(k + 1) * 1408]
        for j in range(IT):
            gi = gu_img(g_rows[j * 128:(j + 1) * 128])
            ui = gu_img(u_rows[j * 128:(j + 1) * 128])
            wgu[u, j] = np.concatenate([gi, ui], axis=2).reshape(128, 4096)
            wdn[u, j] = dT[j * 128:(j + 1) * 128]

    ident = np.eye(128, dtype=ml_dtypes.bfloat16)
    ones = np.ones((1, 128), dtype=ml_dtypes.bfloat16)
    masks = np.empty((128, 4 * 512), dtype=ml_dtypes.bfloat16)
    p_i = np.arange(128)[:, None]
    c_i = np.arange(512)[None, :]
    for r in range(4):
        masks[:, r * 512:(r + 1) * 512] = np.where(
            c_i <= 128 * r + p_i, 0.0, NEGINF).astype(ml_dtypes.bfloat16)

    shared = dict(wqkv=wqkv, wqkvb=wqkvb, wout=wout, woutb=woutb, wrt=wrt,
                  wgu=wgu, wdn=wdn, ident=ident, ones=ones, masks=masks)
    x = np.asarray(L0_attn_x, np.float32)
    return [dict(shared, x=f(x[b])) for b in range(8)]


_NC_CACHE = {}


def kernel(**inputs):
    in_maps = _prep_inputs(**inputs)
    if "prog" not in _NC_CACHE:
        _NC_CACHE["prog"] = build_program(dbg=False)
    nc = _NC_CACHE["prog"]
    res = run_bass_kernel_spmd(nc, in_maps, core_ids=list(range(8)))
    return np.stack([res.results[b]["y"] for b in range(8)], axis=0)
